# version 06
# baseline (speedup 1.0000x reference)
import numpy as np
import ml_dtypes
import concourse.bass as bass
import concourse.mybir as mybir
from concourse.bass_utils import run_bass_kernel_spmd

F32 = mybir.dt.float32
BF16 = mybir.dt.bfloat16
AF = mybir.ActivationFunctionType
ALU = mybir.AluOpType
BFD = ml_dtypes.bfloat16

T = 4096
D = 1024
H = 16
DH = 64
HP = 8
DC = 8
NG = 8
GT = 512
W = 256
NL = 6
DI = 2730
NJ = 22
DIP = NJ * 128
EPS = 1e-5
INVD = 1.0 / D

PARANOID = False
DBG_DUMP = False
DBG_GROUPS = (4, 6)
W_PREFETCH = 16
ENGINES = ("PE", "ACT", "DVE", "PL", "SP")
ENG_ATTR = {"PE": "tensor", "ACT": "scalar", "DVE": "vector", "PL": "gpsimd", "SP": "sync"}



def host_prepare(inp):
    d = {}
    perm = np.concatenate([np.arange(0, 64, 2), np.arange(1, 64, 2)])
    qperm = np.concatenate([h * 64 + perm for h in range(H)])
    Wqkv = np.asarray(inp["L0_attn_Wqkv"], dtype=np.float32)
    d["oWq"] = np.ascontiguousarray(Wqkv[0:1024, :].T[:, qperm])
    d["oWk"] = np.ascontiguousarray(Wqkv[1024:2048, :].T[:, qperm])
    d["oWv"] = np.ascontiguousarray(Wqkv[2048:3072, :].T)
    d["oWo"] = np.ascontiguousarray(np.asarray(inp["L0_attn_Wo"], dtype=np.float32).T)

    d["iWqkv"] = np.ascontiguousarray(inp["L0_ffn_Wqkv"], dtype=np.float32)
    d["iWo"] = np.ascontiguousarray(inp["L0_ffn_Wo"], dtype=np.float32)
    W1 = np.asarray(inp["L0_ffn_W1"], dtype=np.float32)
    W2 = np.asarray(inp["L0_ffn_W2"], dtype=np.float32)
    b1 = np.asarray(inp["L0_ffn_b1"], dtype=np.float32)
    W1P = np.zeros((NL, D, 2 * DIP), np.float32)
    b1P = np.zeros((NL, 2 * DIP), np.float32)
    W2P = np.zeros((NL, DIP, D), np.float32)
    for j in range(NJ):
        lo, hi = j * 128, min((j + 1) * 128, DI)
        n = hi - lo
        W1P[:, :, 2 * j * 128:2 * j * 128 + n] = W1[:, :, lo:hi]
        W1P[:, :, (2 * j + 1) * 128:(2 * j + 1) * 128 + n] = W1[:, :, DI + lo:DI + hi]
        b1P[:, 2 * j * 128:2 * j * 128 + n] = b1[:, lo:hi]
        b1P[:, (2 * j + 1) * 128:(2 * j + 1) * 128 + n] = b1[:, DI + lo:DI + hi]
        W2P[:, j * 128:j * 128 + n, :] = W2[:, lo:hi, :]
    d["W1P"] = W1P
    d["W2P"] = W2P
    d["b1c"] = np.ascontiguousarray(b1P.reshape(NL, 2 * NJ, 128).transpose(0, 2, 1))
    d["b2c"] = np.ascontiguousarray(
        np.asarray(inp["L0_ffn_b2"], dtype=np.float32).reshape(NL, DC, 128).transpose(0, 2, 1))
    d["gc"] = np.ascontiguousarray(
        np.asarray(inp["L0_ffn_ln_g"], dtype=np.float32).reshape(NL, DC, 128).transpose(0, 2, 1))
    d["bcb"] = np.ascontiguousarray(
        np.asarray(inp["L0_ffn_ln_b"], dtype=np.float32).reshape(NL, DC, 128).transpose(0, 2, 1))

    qs = np.asarray(inp["L0_ffn_q_scale"], dtype=np.float32)
    ks = np.asarray(inp["L0_ffn_k_scale"], dtype=np.float32)
    qsI = np.zeros((NL, DC, 16, 128), np.float32)
    ksI = np.zeros((NL, DC, 16, 128), np.float32)
    normI = np.zeros((DC, 128, 16), np.float32)
    for c in range(DC):
        for beta in range(2):
            h = 2 * c + beta
            cols = slice(beta * 64, beta * 64 + 64)
            qsI[:, c, h, cols] = qs
            ksI[:, c, h, cols] = ks
            normI[c, cols, h] = 1.0
    d["qsI"] = np.ascontiguousarray(qsI.transpose(0, 2, 1, 3).reshape(NL, 16, DC * 128))
    d["ksI"] = np.ascontiguousarray(ksI.transpose(0, 2, 1, 3).reshape(NL, 16, DC * 128))
    d["normI"] = np.ascontiguousarray(normI.transpose(1, 0, 2).reshape(128, DC * 16)).astype(BFD)

    t = np.arange(T, dtype=np.float64)
    freqs = 10000.0 ** (-np.arange(0, DH, 2, dtype=np.float64) / DH)
    ang = t[:, None] * freqs[None, :]
    cos_h = np.cos(ang).T
    sin_h = np.sin(ang).T

    def stack2(a):
        b = np.concatenate([a, a], axis=0)
        return np.ascontiguousarray(np.concatenate([b, b], axis=0))

    d["cos2o"] = stack2(cos_h).astype(BFD)
    d["sin2o"] = stack2(sin_h).astype(BFD)
    half = np.arange(0, DH, 2, dtype=np.float64)
    base = (half + 0.4 * DH) / (1.4 * DH)
    power = (t - T // 2) / (W // 2)
    sc = (base[None, :] ** power[:, None]).T
    d["csq2"] = stack2(cos_h * sc).astype(BFD)
    d["ssq2"] = stack2(sin_h * sc).astype(BFD)
    d["csk2"] = stack2(cos_h / sc).astype(BFD)
    d["ssk2"] = stack2(sin_h / sc).astype(BFD)

    R = np.zeros((128, 128), np.float32)
    for b0 in (0, 64):
        for i in range(32):
            R[b0 + i, b0 + 32 + i] = -1.0
            R[b0 + 32 + i, b0 + i] = 1.0
    d["RT"] = np.ascontiguousarray(R.T).astype(BFD)

    mo = np.zeros((4, 128, GT), np.float32)
    for j in range(4):
        for kl in range(128):
            mo[j, kl, j * 128 + kl:] = 3.0e38
    d["mask_o"] = mo.astype(BFD)
    mi = np.zeros((4, 128, W), np.float32)
    for kt in range(4):
        for kl in range(128):
            kw = kt * 128 + kl
            lo = max(0, kw - W)
            mi[kt, kl, lo:min(W, kw + 1)] = 3.0e38
    d["mask_i"] = np.ascontiguousarray(mi.transpose(1, 0, 2).reshape(128, 4 * W)).astype(BFD)

    d["ones_col"] = np.ones((128, 1), np.float32).astype(BFD)
    d["one_row"] = np.ones((1, 128), np.float32).astype(BFD)
    d["neg_row"] = (-np.ones((1, 128), np.float32)).astype(BFD)
    d["biasc"] = np.ascontiguousarray(
        np.tile(np.array([EPS, 1e-24, 1e-37, 0.0], np.float32), (128, 1)))
    return d



class Op:
    __slots__ = ("eng", "fn", "seq", "dbg", "deps", "needs_inc", "dma_sem", "inc",
                 "epoch", "kind", "gid")
    _gid = [0]

    def __init__(self, eng, fn, seq, dbg="", epoch=0):
        self.eng = eng
        self.fn = fn
        self.seq = seq
        self.dbg = dbg
        self.deps = []
        self.needs_inc = False
        self.dma_sem = None
        self.inc = None
        self.epoch = epoch
        self.kind = "op"
        Op._gid[0] += 1
        self.gid = Op._gid[0]


class Res:
    __slots__ = ("name", "writer", "readers")

    def __init__(self, name):
        self.name = name
        self.writer = None
        self.readers = []


class Prog:
    def __init__(self, nc):
        self.nc = nc
        self.ops = {e: [] for e in ENGINES}
        self.res = {}
        self.seq = {e: 0 for e in ENGINES}
        self.dma_streams = {}
        self.last_dma = {}
        self.epoch = 0

    def r(self, name):
        if name not in self.res:
            self.res[name] = Res(name)
        return self.res[name]

    def op(self, eng, fn, reads=(), writes=(), dma=None, dbg=""):
        self.seq[eng] += 1
        o = Op(eng, fn, self.seq[eng], dbg, epoch=self.epoch)
        if dma is not None:
            if dma in self.dma_streams:
                assert self.dma_streams[dma] == eng, (dma, eng, self.dma_streams[dma])
            else:
                self.dma_streams[dma] = eng
            o.dma_sem = dma
            self.last_dma[dma] = o
        deps = []
        for rn in reads:
            rr = self.r(rn)
            if rr.writer is not None:
                deps.append(rr.writer)
        for wn in writes:
            rr = self.r(wn)
            if rr.writer is not None:
                deps.append(rr.writer)
            deps.extend(rr.readers)
        seen = set()
        for p in deps:
            if p is o or id(p) in seen:
                continue
            seen.add(id(p))
            same = (p.eng == eng) and (p.dma_sem is None) and (o.dma_sem is None)
            if same and eng != "PL":
                continue
            o.deps.append(p)
        self.ops[eng].append(o)
        for rn in reads:
            self.r(rn).readers.append(o)
        for wn in writes:
            rr = self.r(wn)
            rr.writer = o
            rr.readers = []
        return o

    def barrier(self):
        lasts = {e: (self.ops[e][-1] if self.ops[e] else None) for e in ENGINES}
        dma_lasts = list(self.last_dma.values())
        self.epoch += 1
        k = self.epoch
        f1 = {}
        for e in ENGINES:
            self.seq[e] += 1
            o = Op(e, lambda eng: eng.nop(), self.seq[e], f"fence1_{k}", epoch=k - 1)
            o.kind = "fence1"
            for e2 in ENGINES:
                if e2 != e and lasts[e2] is not None:
                    o.deps.append(lasts[e2])
            for dl in dma_lasts:
                o.deps.append(dl)
            self.ops[e].append(o)
            f1[e] = o
        self.seq["SP"] += 1
        oc = Op("SP", None, self.seq["SP"], f"semclear_{k}", epoch=k - 1)
        oc.kind = "clear"
        self.ops["SP"].append(oc)
        fences = {}
        for e in ENGINES:
            self.seq[e] += 1
            o = Op(e, lambda eng: eng.nop(), self.seq[e], f"fence2_{k}", epoch=k)
            o.kind = "fence2"
            self.ops[e].append(o)
            fences[e] = o
        for rr in self.res.values():
            if rr.writer is not None:
                rr.writer = fences[rr.writer.eng]
            rr.readers = []
        self.last_dma = {}
        return fences

    def finalize_and_render(self):
        for e in ENGINES:
            for o in self.ops[e]:
                for p in o.deps:
                    if p.dma_sem is None and p.kind in ("op", "fence2"):
                        p.needs_inc = True
        sems = {}
        counts = {}
        maxcount = {}

        def getsem(name):
            if name not in sems:
                sems[name] = self.nc.alloc_semaphore(name)
            return sems[name]

        for e in ENGINES:
            for o in self.ops[e]:
                if o.dma_sem is not None:
                    getsem(o.dma_sem)
                    key = (o.dma_sem, o.epoch)
                    counts[key] = counts.get(key, 0) + 16
                    o.inc = (o.dma_sem, counts[key])
                    maxcount[o.dma_sem] = max(maxcount.get(o.dma_sem, 0), counts[key])
                elif o.needs_inc:
                    sn = "e_" + e
                    getsem(sn)
                    key = (sn, o.epoch)
                    counts[key] = counts.get(key, 0) + 1
                    o.inc = (sn, counts[key])
                    maxcount[sn] = max(maxcount.get(sn, 0), counts[key])
        for snm, mx in maxcount.items():
            assert mx < 30000, ("semaphore epoch count too large", snm, mx)
        allops = sorted((o for e in ENGINES for o in self.ops[e]),
                        key=lambda o: o.gid)
        needmap = {}
        run = {e: {} for e in ENGINES}
        for o in allops:
            nm = dict(run[o.eng])
            for p in o.deps:
                pm = needmap.get(id(p))
                if pm:
                    for k2, v2 in pm.items():
                        if nm.get(k2, 0) < v2:
                            nm[k2] = v2
                if p.dma_sem is not None and o.dma_sem is None and o.kind == "op":
                    k2 = (p.dma_sem, p.epoch)
                    if nm.get(k2, 0) < p.inc[1]:
                        nm[k2] = p.inc[1]
            needmap[id(o)] = nm
            run[o.eng] = {} if o.kind == "fence2" else nm
        for nm in needmap.values():
            for (s, _ep) in nm:
                getsem("k_" + s)
        gate = self.nc.alloc_semaphore("gate")
        episem = self.nc.alloc_semaphore("episem")
        data_sems = dict(sems)
        nc = self.nc
        stats = {e: dict(ops=len(self.ops[e]), waits=0, incs=0) for e in ENGINES}
        nbar = self.epoch
        with nc.Block() as blk:
            for e in ENGINES:
                def body(eng_obj, e=e):
                    hw = {}
                    nclear = [0]
                    laund = {}
                    for o in self.ops[e]:
                        if o.kind == "clear":
                            nclear[0] += 1
                            k = nclear[0]
                            eng_obj.wait_ge(gate, 5 * k)
                            for snm in sorted(data_sems):
                                eng_obj.sem_clear(data_sems[snm])
                            eng_obj.sem_inc(episem, 1)
                            continue
                        for key, v in sorted(needmap[id(o)].items()):
                            s = key[0]
                            if self.dma_streams[s] != e or v <= laund.get(key, 0):
                                continue
                            if hw.get(s, 0) < v:
                                eng_obj.wait_ge(sems[s], v)
                                hw[s] = v
                                stats[e]["waits"] += 1
                            eng_obj.nop().then_inc(sems["k_" + s],
                                                   v - laund.get(key, 0))
                            laund[key] = v
                            stats[e]["incs"] += 1
                        wd = {}
                        for p in o.deps:
                            assert p.inc is not None, (e, o.dbg, p.eng, p.dbg)
                            s, v = p.inc
                            if (o.kind == "op" and o.dma_sem is None
                                    and p.dma_sem is not None):
                                s = "k_" + s
                            wd[s] = max(wd.get(s, 0), v)
                        for s, v in sorted(wd.items()):
                            if hw.get(s, 0) >= v:
                                continue
                            eng_obj.wait_ge(sems[s], v)
                            hw[s] = v
                            stats[e]["waits"] += 1
                        if o.kind == "fence1":
                            ins = o.fn(eng_obj)
                            ins.then_inc(gate, 1)
                            continue
                        if o.kind == "fence2":
                            eng_obj.wait_ge(episem, o.epoch)
                            hw.clear()
                            ins = o.fn(eng_obj)
                            if o.inc is not None:
                                s, _v = o.inc
                                ins.then_inc(sems[s], 1)
                            continue
                        ins = o.fn(eng_obj)
                        if o.inc is not None:
                            s, _v = o.inc
                            assert ins is not None, ("inc needs instruction", e, o.dbg)
                            ins.then_inc(sems[s], 16 if o.dma_sem is not None else 1)
                            stats[e]["incs"] += 1

                getattr(blk, ENG_ATTR[e])(body)
        return stats



class Ctx:
    pass


def build_program(n_layers=NL, with_outer=True, dev_io=False):
    nc = bass.Bass("TRN2", target_bir_lowering=False, debug=False,
                   detect_race_conditions=False)
    P = Prog(nc)
    c = Ctx()
    c.nc, c.P = nc, P
    c.n_layers = n_layers
    tn = {}

    def din(name, shape, dtype=F32):
        tn[name] = nc.dram_tensor(name, list(shape), dtype, kind="ExternalInput").ap()

    def dint(name, shape, dtype=F32):
        kind = "Internal"
        if dev_io and name in ("Y0", "X1"):
            kind = "ExternalInput"
        if dev_io and name in ("Y1",):
            kind = "ExternalOutput"
        tn[name] = nc.dram_tensor(name, list(shape), dtype, kind=kind).ap()

    din("xT", (D, T))
    din("oWq", (D, D)); din("oWk", (D, D)); din("oWv", (D, D)); din("oWo", (D, D))
    din("iWqkv", (NL, D, 3 * D)); din("iWo", (NL, D, D))
    din("W1P", (NL, D, 2 * DIP)); din("W2P", (NL, DIP, D))
    din("b1c", (NL, 128, 2 * NJ)); din("b2c", (NL, 128, DC))
    din("gc", (NL, 128, DC)); din("bcb", (NL, 128, DC))
    din("qsI", (NL, 16, DC * 128)); din("ksI", (NL, 16, DC * 128))
    din("normI", (128, DC * 16), BF16)
    din("cos2o", (128, T), BF16); din("sin2o", (128, T), BF16)
    din("csq2", (128, T), BF16); din("ssq2", (128, T), BF16)
    din("csk2", (128, T), BF16); din("ssk2", (128, T), BF16)
    din("RT", (128, 128), BF16)
    din("mask_o", (4, 128, GT), BF16); din("mask_i", (128, 4 * W), BF16)
    din("ones_col", (128, 1), BF16); din("one_row", (1, 128), BF16)
    din("neg_row", (1, 128), BF16)
    din("biasc", (128, 4))
    tn["outT"] = nc.dram_tensor("outT", [D, T], F32, kind="ExternalOutput").ap()

    dint("X1", (D, T)); dint("Y0", (D, T)); dint("Y1", (D, T))
    if dev_io:
        tn["dbgS"] = nc.dram_tensor("dbgS", [NG, 128, DC * GT], BF16, kind="ExternalOutput").ap()
        tn["dbgF"] = nc.dram_tensor("dbgF", [NG, 128, DC * GT], BF16, kind="ExternalOutput").ap()
        tn["dbgH"] = nc.dram_tensor("dbgH", [NG, 128, NJ * GT], BF16, kind="ExternalOutput").ap()
    c.dev_io = dev_io
    dint("Ksc", (H, DH, T), BF16)
    dint("Vsc", (T // 128, 128, H * 65), BF16)
    dint("bWqkv", (NL, D, 3 * D), BF16)
    dint("bWo", (NL, D, D), BF16)
    dint("bW1P", (NL, D, 2 * DIP), BF16)
    dint("bW2P", (NL, DIP, D), BF16)
    dint("bOW", (4, D, D), BF16)
    c.tn = tn

    sb = {}

    def S(name, shape, dtype):
        sb[name] = nc.alloc_sbuf_tensor(name, list(shape), dtype)

    S("yin0", (128, DC * (GT + 1)), F32)
    S("yin1", (128, DC * (GT + 1)), F32)
    S("yout", (128, DC * GT), F32)
    S("sbf", (128, DC * GT), BF16)
    S("sq", (128, DC * GT), BF16)
    S("wr0", (128, 20 * 128), BF16)
    S("wr1", (128, 8192), BF16)
    S("ctabs", (128, 4 * GT), BF16)
    S("wcast", (128, 2 * 1024), F32)
    S("wcastb", (128, 2 * 1024), BF16)
    S("arBF", (128, 29824), BF16)
    S("arF32", (128, 4 * 1024), F32)
    S("rowf0", (128, GT), F32)
    S("rowf1", (128, GT), F32)
    S("rowb", (128, GT), BF16)
    S("rn32", (16, GT), F32)
    S("rnb", (16, GT), BF16)
    S("u0", (128, GT), BF16)
    S("w0", (128, GT), BF16)
    S("abuf", (128, GT), BF16)
    S("gbuf", (128, GT), BF16)
    S("Ost", (64, GT), F32)
    S("bcAt", (128, GT), BF16)
    S("bcBt", (128, GT), BF16)
    S("cones_col", (128, 1), BF16)
    S("cone_row", (1, 128), BF16)
    S("cneg_row", (1, 128), BF16)
    S("cRT", (128, 128), BF16)
    S("cmask_i", (128, 4 * W), BF16)
    S("cbias", (128, 4), F32)
    S("cnormI", (128, DC * 16), BF16)
    S("cqsI", (16, DC * 128), BF16)
    S("cksI", (16, DC * 128), BF16)
    S("cb1", (128, 2 * NJ), F32)
    S("cb2", (128, DC), F32)
    S("cg", (128, DC), F32)
    S("cbb", (128, DC), F32)
    c.sb = sb
    c.mu = sb["rowf0"][0:1, :]
    c.qb = sb["rowf0"][32:33, :]
    c.mu2 = sb["rowf1"][32:33, :]
    c.var = sb["rowf0"][64:65, :]
    c.lnv = sb["rowf1"][0:1, :]
    c.dnm = sb["rowf1"][64:65, :]
    c.rcp = sb["rowf1"][96:97, :]
    c.rstdb = sb["rowb"][0:1, :]
    c.rcpb = sb["rowb"][0:1, :]
    c.mrstdb = sb["rnb"][0:1, :]

    A = sb["arBF"]
    c.qsc = A[:, 0:HP * GT]
    c.qrope = A[:, 4096:4096 + HP * GT]
    c.kring = A[:, 8192:8192 + HP * 1024]
    vb = 16384
    c.vring = [A[:, vb + i * 1040: vb + (i + 1) * 1040] for i in range(8)]
    kb = 8192
    c.krp = A[:, kb:kb + GT]
    c.kvst = [A[:, kb + GT + i * GT: kb + GT + (i + 1) * GT] for i in range(2)]
    c.cmask_o = [A[:, kb + 3 * GT + i * GT: kb + 3 * GT + (i + 1) * GT] for i in range(4)]
    c.vst = [A[:, vb + i * 1040: vb + (i + 1) * 1040] for i in range(2)]
    pb = vb + 8320
    c.Pt = A[:, pb:pb + 4 * W]
    wb = pb + 1024
    c.WoIn = A[:, wb:wb + HP * GT]
    c.s2n = A[:, 0:DC * GT]
    c.hbuf = A[:, 4096:4096 + NJ * GT]
    c.w2res = A[:, 4096 + NJ * GT:4096 + NJ * GT + 88 * 128]
    AF32 = sb["arF32"]
    c.zb = AF32[:, 0:DC * GT]
    c.xb = AF32[:, 0:DC * GT]

    ps = {}
    for n in ("pA", "pB", "pC", "pD", "pS0", "pS1", "pO", "pR"):
        ps[n] = nc.alloc_psum_tensor(n, [128, GT], F32)
    c.ps = ps

    emit_all(c, n_layers, with_outer)
    stats = P.finalize_and_render()
    return nc, stats



def mmop(c, psum_ap, lhsT_ap, rhs_ap, start, stop, reads, writes, dbg, tp=None):
    if tp is None:
        c.P.op("PE", lambda e, o=psum_ap, w=lhsT_ap, r=rhs_ap, s=start, t=stop:
               (e.ldweights(w), e.matmul(out=o, lhsT=w, rhs=r, start=s, stop=t))[1],
               reads=reads, writes=writes, dbg=dbg)
    else:
        c.P.op("PE", lambda e, o=psum_ap, w=lhsT_ap, r=rhs_ap, s=start, t=stop, tp_=tp:
               (e.ldweights(w, tile_position=tp_),
                e.matmul(out=o, lhsT=w, rhs=r, start=s, stop=t, tile_position=tp_))[1],
               reads=reads, writes=writes, dbg=dbg)


def yin_ap(c, buf, ci, shifted, n=GT):
    y = c.sb[f"yin{buf}"]
    off = ci * (GT + 1) + (0 if shifted else 1)
    return y[:, off:off + n]


def emit_yin_load(c, buf, g, src_name, tag):
    P, tn, sb = c.P, c.tn, c.sb
    src = tn[src_name]
    srcres = [] if src_name == "xT" else ([f"{src_name}_g{g}"] + ([f"{src_name}_g{g-1}"] if g > 0 else []))
    for ci in range(DC):
        if g == 0:
            P.op("DVE", lambda e, a=sb[f"yin{buf}"][:, ci * (GT + 1):ci * (GT + 1) + 1]:
                 e.memset(a, 0.0), reads=(), writes=(f"yin{buf}_{ci}",), dbg=f"{tag}:z{ci}")
            P.op("ACT", lambda e, d=sb[f"yin{buf}"][:, ci * (GT + 1) + 1:(ci + 1) * (GT + 1)],
                 s=src[ci * 128:(ci + 1) * 128, 0:GT]: e.dma_start(out=d, in_=s),
                 reads=tuple(srcres), writes=(f"yin{buf}_{ci}",), dma="d_y", dbg=f"{tag}:l{ci}")
        else:
            P.op("ACT", lambda e, d=sb[f"yin{buf}"][:, ci * (GT + 1):(ci + 1) * (GT + 1)],
                 s=src[ci * 128:(ci + 1) * 128, g * GT - 1:(g + 1) * GT]: e.dma_start(out=d, in_=s),
                 reads=tuple(srcres), writes=(f"yin{buf}_{ci}",), dma="d_y", dbg=f"{tag}:l{ci}")


def emit_sbf_cast(c, buf, tag, do_shift=True):
    P, sb = c.P, c.sb
    for ci in range(DC):
        sh = do_shift and ci >= DC // 2
        ap = yin_ap(c, buf, ci, sh)
        P.op("DVE", lambda e, a=ap, d=sb["sbf"][:, ci * GT:(ci + 1) * GT]:
             e.tensor_copy(out=d, in_=a),
             reads=(f"yin{buf}_{ci}",), writes=(f"sbf{ci}",), dbg=f"{tag}:c{ci}")


def ln_compute(c, chunks, tag):
    P, sb, ps = c.P, c.sb, c.ps
    sqt = sb["sq"]
    for i, (ap, rr) in enumerate(chunks):
        P.op("DVE", lambda e, a=ap, d=sqt[:, i * GT:(i + 1) * GT]:
             e.tensor_tensor(out=d, in0=a, in1=a, op=ALU.mult),
             reads=(rr,), writes=(f"sq{i}",), dbg=f"{tag}:sq{i}")
    for i, (ap, rr) in enumerate(chunks):
        mmop(c, ps["pR"][0:1, 0:GT], sb["cones_col"][:], ap, i == 0, i == DC - 1,
             (rr, "cones_col", "pR"), ("pR",), f"{tag}:sum{i}")
    P.op("ACT", lambda e: e.activation(out=c.mu, in_=ps["pR"][0:1, 0:GT],
                                       func=AF.Identity, bias=0.0, scale=INVD),
         reads=("pR",), writes=("st_mu",), dbg=f"{tag}:mu")
    for i in range(DC):
        mmop(c, ps["pO"][0:1, 0:GT], sb["cones_col"][:], sqt[:, i * GT:(i + 1) * GT],
             i == 0, i == DC - 1, (f"sq{i}", "cones_col", "pO"), ("pO",), f"{tag}:ssq{i}")
    P.op("ACT", lambda e: e.activation(out=c.qb, in_=ps["pO"][0:1, 0:GT],
                                       func=AF.Identity, bias=0.0, scale=INVD),
         reads=("pO",), writes=("st_q",), dbg=f"{tag}:qb")
    P.op("DVE", lambda e: e.tensor_tensor(out=c.mu2, in0=c.mu, in1=c.mu, op=ALU.mult),
         reads=("st_mu",), writes=("st_mu2",), dbg=f"{tag}:mu2")
    P.op("DVE", lambda e: e.tensor_tensor(out=c.var, in0=c.qb, in1=c.mu2, op=ALU.subtract),
         reads=("st_q", "st_mu2"), writes=("st_var",), dbg=f"{tag}:var")
    P.op("ACT", lambda e: e.activation(out=c.lnv, in_=c.var,
                                       func=AF.Ln, bias=c.sb["cbias"][0:1, 0:1], scale=1.0),
         reads=("st_var", "cbias"), writes=("st_lnv",), dbg=f"{tag}:lnv")
    P.op("ACT", lambda e: e.activation(out=c.rstdb, in_=c.lnv,
                                       func=AF.Exp, bias=0.0, scale=-0.5),
         reads=("st_lnv",), writes=("rstdb",), dbg=f"{tag}:rstd")
    P.op("DVE", lambda e: e.tensor_tensor(out=c.mrstdb, in0=c.mu, in1=c.rstdb, op=ALU.mult),
         reads=("st_mu", "rstdb"), writes=("mrstdb",), dbg=f"{tag}:mrstd")
    mmop(c, ps["pR"][:, 0:GT], sb["cone_row"][:], c.rstdb, True, True,
         ("cone_row", "rstdb", "pR"), ("pR",), f"{tag}:bcA")
    P.op("ACT", lambda e: e.activation(out=sb["bcAt"][:], in_=ps["pR"][:, 0:GT],
                                       func=AF.Identity, bias=0.0, scale=1.0),
         reads=("pR",), writes=("bcAt",), dbg=f"{tag}:bcAe")
    mmop(c, ps["pR"][:, 0:GT], sb["cneg_row"][:], c.mrstdb, True, True,
         ("cneg_row", "mrstdb", "pR"), ("pR",), f"{tag}:bcB")
    P.op("ACT", lambda e: e.activation(out=sb["bcBt"][:], in_=ps["pR"][:, 0:GT],
                                       func=AF.Identity, bias=0.0, scale=1.0),
         reads=("pR",), writes=("bcBt",), dbg=f"{tag}:bcBe")


def qpad_ap(c, h):
    A = c.sb["arBF"]
    return A[:, h * GT:(h + 1) * GT]


def emit_rope_padq(c, in_ap, in_res, hp, cs_ap, cs_res, ss_ap, ss_res, tag):
    P, sb, ps = c.P, c.sb, c.ps
    mmop(c, ps["pD"][:, 0:GT], sb["cRT"][:], in_ap, True, True,
         ("cRT", in_res, "pD"), ("pD",), f"{tag}:rot")
    P.op("DVE", lambda e, a=in_ap, t=cs_ap: e.tensor_tensor(out=sb["u0"][:], in0=a, in1=t, op=ALU.mult),
         reads=(in_res, cs_res), writes=("u0",), dbg=f"{tag}:u")
    P.op("DVE", lambda e, t=ss_ap: e.tensor_tensor(out=sb["w0"][:], in0=ps["pD"][:, 0:GT], in1=t, op=ALU.mult),
         reads=("pD", ss_res), writes=("w0",), dbg=f"{tag}:w")
    for beta in range(2):
        h = 2 * hp + beta
        dst = qpad_ap(c, h)
        b0 = beta * 64
        alias = (f"qsc{h}",) if h < HP else (f"qsc_hi{h - HP}",)
        P.op("DVE", lambda e, d=dst[b0:b0 + 64, :], u=sb["u0"][b0:b0 + 64, :],
             w=sb["w0"][b0:b0 + 64, :]: e.tensor_tensor(out=d, in0=u, in1=w, op=ALU.add),
             reads=("u0", "w0"), writes=(f"qpad{h}",) + alias, dbg=f"{tag}:o{beta}")
        z0 = (1 - beta) * 64
        P.op("DVE", lambda e, d=dst[z0:z0 + 64, :]: e.memset(d, 0.0),
             reads=(), writes=(f"qpad{h}",) + alias, dbg=f"{tag}:z{beta}")


def emit_rope(c, in_ap, in_res, out_ap, out_res, cs_ap, cs_res, ss_ap, ss_res, tag):
    P, sb, ps = c.P, c.sb, c.ps
    mmop(c, ps["pD"][:, 0:GT], sb["cRT"][:], in_ap, True, True,
         ("cRT", in_res, "pD"), ("pD",), f"{tag}:rot")
    P.op("DVE", lambda e, a=in_ap, t=cs_ap: e.tensor_tensor(out=sb["u0"][:], in0=a, in1=t, op=ALU.mult),
         reads=(in_res, cs_res), writes=("u0",), dbg=f"{tag}:u")
    P.op("DVE", lambda e, t=ss_ap: e.tensor_tensor(out=sb["w0"][:], in0=ps["pD"][:, 0:GT], in1=t, op=ALU.mult),
         reads=("pD", ss_res), writes=("w0",), dbg=f"{tag}:w")
    P.op("DVE", lambda e, d=out_ap: e.tensor_tensor(out=d, in0=sb["u0"][:], in1=sb["w0"][:], op=ALU.add),
         reads=("u0", "w0"), writes=(out_res,), dbg=f"{tag}:o")



def weight_cast_jobs(c, n_layers):
    tn = c.tn
    jobs = [(tn["oWq"], tn["bOW"][0], D, D), (tn["oWk"], tn["bOW"][1], D, D),
            (tn["oWv"], tn["bOW"][2], D, D), (tn["oWo"], tn["bOW"][3], D, D)]
    for li in range(n_layers):
        jobs.append((tn["iWqkv"][li], tn["bWqkv"][li], D, 3 * D, f"bWqkv{li}"))
        jobs.append((tn["iWo"][li], tn["bWo"][li], D, D, f"bWo{li}"))
        jobs.append((tn["W1P"][li], tn["bW1P"][li], D, 2 * DIP, f"bW1P{li}"))
        jobs.append((tn["W2P"][li], tn["bW2P"][li], DIP, D, f"bW2P{li}"))
    names = ["bOW0", "bOW1", "bOW2", "bOW3"]
    out = []
    for idx, job in enumerate(jobs):
        if len(job) == 5:
            src, dst, rows, cols, nm = job
        else:
            src, dst, rows, cols = job
            nm = names[idx]
        out.append((src, dst, rows, cols, nm))
    return out


def emit_wc_chunk(c, state):
    P, sb = c.P, c.sb
    wc, wcb = sb["wcast"], sb["wcastb"]
    CH = 1024
    budget = state.pop("budget", 8)
    n = 0
    while state["jobs"] and n < budget:
        src, dst, rows, cols, nm = state["jobs"][0]
        r0, c0 = state["pos"]
        cn = min(CH, cols - c0)
        sl = state["slot"] % 2
        state["slot"] += 1
        fr, br = f"wcf{sl}", f"wcb{sl}"
        P.op("ACT", lambda e, d=wc[:, sl * CH:sl * CH + cn],
             s=src[r0:r0 + 128, c0:c0 + cn]: e.dma_start(out=d, in_=s),
             reads=(br,), writes=(fr,), dma="d_wc", dbg="wcl")
        P.op("PL", lambda e, d=wcb[:, sl * CH:sl * CH + cn],
             s=wc[:, sl * CH:sl * CH + cn]: e.tensor_copy(out=d, in_=s),
             reads=(fr,), writes=(br,), dbg="wcc")
        P.op("ACT", lambda e, d=dst[r0:r0 + 128, c0:c0 + cn],
             s=wcb[:, sl * CH:sl * CH + cn]: e.dma_start(out=d, in_=s),
             reads=(br,), writes=(nm,), dma="d_wcs", dbg="wcs")
        n += 1
        c0 += cn
        if c0 >= cols:
            c0 = 0
            r0 += 128
            if r0 >= rows:
                state["jobs"].pop(0)
                r0 = 0
        state["pos"] = (r0, c0)


def drain_wc(c, state, budget=8):
    state["budget"] = budget
    emit_wc_chunk(c, state)



class WTiles:

    def __init__(self, c, hbm_ap, hbm_res, n_k, oc_list, ring, tag,
                 stream="d_w", prefetch=None):
        if prefetch is None:
            prefetch = W_PREFETCH
        self.c, self.hbm_ap, self.hbm_res = c, hbm_ap, hbm_res
        self.n_k, self.oc_list, self.ring, self.tag = n_k, oc_list, ring, tag
        self.stream, self.prefetch = stream, prefetch
        self.rng = c.sb[ring]
        self.nslots = self.rng.shape[1] // 128
        self.emitted = 0
        self.total = len(oc_list) * n_k

    def _emit_one(self):
        n = self.emitted
        i, k = divmod(n, self.n_k)
        oc = self.oc_list[i]
        t = n % self.nslots
        res = f"{self.ring}_{t}"
        self.c.P.op("SP", lambda e, d=self.rng[:, t * 128:(t + 1) * 128],
                    s=self.hbm_ap[k * 128:(k + 1) * 128, oc * 128:(oc + 1) * 128]:
                    e.dma_start(out=d, in_=s),
                    reads=(self.hbm_res,) if self.hbm_res else (), writes=(res,),
                    dma=self.stream, dbg=f"{self.tag}:wt{i}_{k}")
        self.emitted += 1

    def __call__(self, i, k):
        n = i * self.n_k + k
        assert n < self.total
        last = getattr(self, "_last", -1)
        assert n >= last, ("WTiles consumed out of order", self.tag, n, last)
        self._last = n
        while self.emitted <= min(n + self.prefetch, self.total - 1):
            self._emit_one()
        t = n % self.nslots
        return self.rng[:, t * 128:(t + 1) * 128], f"{self.ring}_{t}"


def stream_w_tiles(c, hbm_ap, hbm_res, n_k, oc_list, ring, tag, stream="d_w"):
    return WTiles(c, hbm_ap, hbm_res, n_k, oc_list, ring, tag, stream)



def emit_pass_C(c):
    P, tn, sb = c.P, c.tn, c.sb
    for g in range(NG):
        buf = g % 2
        emit_yin_load(c, buf, g, "X1", f"C{g}")
        emit_sbf_cast(c, buf, f"C{g}", do_shift=False)
        chunks = [(sb["sbf"][:, i * GT:(i + 1) * GT], f"sbf{i}") for i in range(DC)]
        ln_compute(c, chunks, f"C{g}")
        for ci in range(DC):
            P.op("DVE", lambda e, a=yin_ap(c, buf, ci, False), d=sb["yout"][:, ci * GT:(ci + 1) * GT]:
                 e.tensor_tensor(out=d, in0=a, in1=sb["bcAt"][:], op=ALU.mult),
                 reads=(f"yin{buf}_{ci}", "bcAt"), writes=(f"yo{ci}",), dbg=f"C{g}:m{ci}")
            P.op("DVE", lambda e, d=sb["yout"][:, ci * GT:(ci + 1) * GT]:
                 e.tensor_tensor(out=d, in0=d, in1=sb["bcBt"][:], op=ALU.add),
                 reads=("bcBt", f"yo{ci}"), writes=(f"yo{ci}",), dbg=f"C{g}:a{ci}")
            P.op("ACT", lambda e, d=tn["Y0"][ci * 128:(ci + 1) * 128, g * GT:(g + 1) * GT],
                 s=sb["yout"][:, ci * GT:(ci + 1) * GT]: e.dma_start(out=d, in_=s),
                 reads=(f"yo{ci}",), writes=(f"Y0_g{g}",), dma="d_yo", dbg=f"C{g}:s{ci}")



def emit_inner_attn(c, li, ysrc, ydst):
    P, tn, sb, ps = c.P, c.tn, c.sb, c.ps
    P.op("SP", lambda e: e.dma_start(out=sb["wcast"][0:16, 0:1024], in_=tn["qsI"][li]),
         reads=("wcf0", "wcf1"), writes=("wcf0", "wcf1"), dma="d_sp", dbg=f"ia{li}:qsl")
    P.op("DVE", lambda e: e.tensor_copy(out=sb["cqsI"][:], in_=sb["wcast"][0:16, 0:1024]),
         reads=("wcf0",), writes=("cqsI",), dbg=f"ia{li}:qsc")
    P.op("SP", lambda e: e.dma_start(out=sb["wcast"][0:16, 1024:2048], in_=tn["ksI"][li]),
         reads=("wcf0", "wcf1"), writes=("wcf0", "wcf1"), dma="d_sp", dbg=f"ia{li}:ksl")
    P.op("DVE", lambda e: e.tensor_copy(out=sb["cksI"][:], in_=sb["wcast"][0:16, 1024:2048]),
         reads=("wcf0",), writes=("cksI",), dbg=f"ia{li}:ksc")
    for i in range(8):
        ap = c.vring[i].rearrange("p (h z) -> p h z", z=65)[:, :, 64:65]
        P.op("DVE", lambda e, a=ap: e.memset(a, 1.0), reads=(), writes=(f"vring{i}",), dbg="vr1")

    for g in range(NG):
        buf = g % 2
        gsl = slice(g * GT, (g + 1) * GT)
        emit_yin_load(c, buf, g, ysrc, f"ia{li}g{g}")
        emit_sbf_cast(c, buf, f"ia{li}g{g}")
        for ti, tname in enumerate(("csq2", "ssq2", "csk2", "ssk2")):
            P.op("SP", lambda e, d=sb["ctabs"][:, ti * GT:(ti + 1) * GT],
                 s=tn[tname][:, gsl]: e.dma_start(out=d, in_=s),
                 reads=(), writes=(f"ctab{ti}",), dma="d_tab", dbg=f"ia{li}g{g}:t{ti}")
        wt = stream_w_tiles(c, tn["bWqkv"][li], f"bWqkv{li}", DC,
                            list(range(8, 16)) + list(range(0, 8)),
                            "wr0", f"ia{li}g{g}")
        for isq in (1, 0):
            qk = "qk"[isq]
            hporder = range(HP) if isq == 1 else range(HP - 1, -1, -1)
            for hp in range(HP):
                pn, pres = (ps["pA"], "pA") if hp % 2 == 0 else (ps["pB"], "pB")
                for k in range(DC):
                    ap, res = wt((0 if isq == 1 else 8) + hp, k)
                    mmop(c, pn[:, 0:GT], ap, sb["sbf"][:, k * GT:(k + 1) * GT],
                         k == 0, k == DC - 1, (res, f"sbf{k}", pres), (pres,),
                         f"ia{li}g{g}:{qk}p{hp}")
                P.op("ACT", lambda e, o=c.qsc[:, hp * GT:(hp + 1) * GT], i=pn[:, 0:GT]:
                     e.activation(out=o, in_=i, func=AF.Identity, bias=0.0, scale=1.0),
                     reads=(pres,), writes=(f"qsc{hp}", f"qpad{hp}"), dbg=f"ia{li}g{g}:{qk}e{hp}")
                P.op("DVE", lambda e, a=c.qsc[:, hp * GT:(hp + 1) * GT],
                     d=sb["sq"][:, hp * GT:(hp + 1) * GT]:
                     e.tensor_tensor(out=d, in0=a, in1=a, op=ALU.mult),
                     reads=(f"qsc{hp}",), writes=(f"sq{hp}",), dbg=f"ia{li}g{g}:{qk}s{hp}")
            for k in range(DC):
                mmop(c, ps["pR"][0:16, 0:GT], sb["cnormI"][:, k * 16:(k + 1) * 16], sb["sq"][:, k * GT:(k + 1) * GT],
                     k == 0, k == DC - 1, ("cnormI", f"sq{k}", "pR"), ("pR",),
                     f"ia{li}g{g}:{qk}rn{k}")
            P.op("ACT", lambda e: e.activation(out=sb["rn32"][:], in_=ps["pR"][0:16, 0:GT],
                                               func=AF.Ln, bias=sb["cbias"][0:16, 1:2], scale=1.0),
                 reads=("pR", "cbias"), writes=("rn32",), dbg=f"ia{li}g{g}:{qk}ln")
            P.op("ACT", lambda e: e.activation(out=sb["rnb"][0:16, :], in_=sb["rn32"][:],
                                               func=AF.Exp, bias=0.0, scale=-0.5),
                 reads=("rn32",), writes=("rnb",), dbg=f"ia{li}g{g}:{qk}rne")
            iI = "cqsI" if isq == 0 else "cksI"
            cs_i, ss_i = (0, 1) if isq == 0 else (2, 3)
            for hp in hporder:
                mmop(c, ps["pC"][:, 0:GT], sb[iI][:, hp * 128:(hp + 1) * 128], sb["rnb"][0:16, :],
                     True, True, (iI, "rnb", "pC"), ("pC",), f"ia{li}g{g}:{qk}b{hp}")
                P.op("DVE", lambda e, a=c.qsc[:, hp * GT:(hp + 1) * GT]:
                     e.tensor_tensor(out=a, in0=a, in1=ps["pC"][:, 0:GT], op=ALU.mult),
                     reads=(f"qsc{hp}", "pC"), writes=(f"qsc{hp}", f"qpad{hp}"), dbg=f"ia{li}g{g}:{qk}m{hp}")
                if isq == 1:
                    out_ap = c.kring[:, hp * 1024 + (g % 2) * GT: hp * 1024 + (g % 2) * GT + GT]
                    emit_rope(c, c.qsc[:, hp * GT:(hp + 1) * GT], f"qsc{hp}", out_ap,
                              f"kring{g % 2}_{hp}",
                              sb["ctabs"][:, cs_i * GT:(cs_i + 1) * GT], f"ctab{cs_i}",
                              sb["ctabs"][:, ss_i * GT:(ss_i + 1) * GT], f"ctab{ss_i}",
                              f"ia{li}g{g}:{qk}r{hp}")
                else:
                    emit_rope_padq(c, c.qsc[:, hp * GT:(hp + 1) * GT], f"qsc{hp}", hp,
                                   sb["ctabs"][:, cs_i * GT:(cs_i + 1) * GT], f"ctab{cs_i}",
                                   sb["ctabs"][:, ss_i * GT:(ss_i + 1) * GT], f"ctab{ss_i}",
                                   f"ia{li}g{g}:{qk}r{hp}")
        for k in range(DC):
            P.op("SP", lambda e, d=sb["wr1"][:, k * 1024:(k + 1) * 1024],
                 s=tn["bWqkv"][li][k * 128:(k + 1) * 128, 2048:3072]: e.dma_start(out=d, in_=s),
                 reads=(f"bWqkv{li}",), writes=(f"wr1_{k}",), dma="d_w", dbg=f"ia{li}g{g}:wv{k}")
        for sub in range(4):
            vidx = (4 * g + sub) % 8
            for half in range(2):
                pn, pres = (ps["pS0"], "pS0") if half == 0 else (ps["pS1"], "pS1")
                for k in range(DC):
                    mmop(c, pn[:, 0:GT],
                         sb["sbf"][:, k * GT + sub * 128:k * GT + (sub + 1) * 128],
                         sb["wr1"][:, k * 1024 + half * GT:k * 1024 + (half + 1) * GT],
                         k == 0, k == DC - 1,
                         (f"sbf{k}", f"wr1_{k}", pres), (pres,), f"ia{li}g{g}:v{sub}_{half}")
                src = pn[:, 0:GT].rearrange("p (h z) -> p h z", z=64)
                dst = c.vring[vidx].rearrange("p (h z) -> p h z", z=65)[:, half * 8:(half + 1) * 8, 0:64]
                P.op("DVE", lambda e, s=src, d=dst: e.tensor_copy(out=d, in_=s),
                     reads=(pres,), writes=(f"vring{vidx}",), dbg=f"ia{li}g{g}:ve{sub}_{half}")
        for wloc in range(2):
            w = 2 * g + wloc
            qoff = wloc * W
            kts = [2, 3] if w == 0 else [0, 1, 2, 3]
            for h in range(H):
                hp, beta = h // 2, h % 2
                b0 = beta * 64
                kreads = [f"kring{0}_{hp}", f"kring{1}_{hp}"]
                for kt in kts:
                    tk = (w - 1) * W + kt * 128
                    col = tk % 1024
                    pn, pres = (ps["pS0"], "pS0") if kt < 2 else (ps["pS1"], "pS1")
                    mmop(c, pn[:, (kt % 2) * W:(kt % 2 + 1) * W],
                         c.kring[:, hp * 1024 + col:hp * 1024 + col + 128],
                         qpad_ap(c, h)[:, qoff:qoff + W],
                         True, True,
                         tuple(kreads) + (f"qpad{h}", pres), (pres,),
                         f"ia{li}g{g}:S{w}_{h}_{kt}")
                if w != 0:
                    P.op("ACT", lambda e: e.activation(out=c.Pt[:, 0:2 * W], in_=ps["pS0"][:, 0:2 * W],
                                                       func=AF.Exp, bias=0.0, scale=8.0),
                         reads=("pS0",), writes=("Pt01",), dbg=f"ia{li}g{g}:x{w}_{h}a")
                P.op("ACT", lambda e: e.activation(out=c.Pt[:, 2 * W:4 * W], in_=ps["pS1"][:, 0:2 * W],
                                                   func=AF.Exp, bias=0.0, scale=8.0),
                     reads=("pS1",), writes=("Pt23",), dbg=f"ia{li}g{g}:x{w}_{h}b")
                for kt in kts:
                    pt_res = "Pt01" if kt < 2 else "Pt23"
                    P.op("DVE", lambda e, a=c.Pt[:, kt * W:(kt + 1) * W], m=sb["cmask_i"][:, kt * W:(kt + 1) * W]:
                         e.tensor_tensor(out=a, in0=a, in1=m, op=ALU.min),
                         reads=(pt_res, "cmask_i"), writes=(pt_res,), dbg=f"ia{li}g{g}:m{w}_{h}_{kt}")
                for ki, kt in enumerate(kts):
                    vidx = ((w - 1) * 2 + kt) % 8
                    pt_res = "Pt01" if kt < 2 else "Pt23"
                    mmop(c, ps["pO"][0:65, 0:W],
                         c.vring[vidx][:, h * 65:(h + 1) * 65],
                         c.Pt[:, kt * W:(kt + 1) * W],
                         ki == 0, ki == len(kts) - 1,
                         (f"vring{vidx}", pt_res, "pO"), ("pO",), f"ia{li}g{g}:A{w}_{h}_{kt}")
                P.op("ACT", lambda e: e.activation(out=c.dnm[:, 0:W], in_=ps["pO"][64:65, 0:W],
                                                   func=AF.Identity, bias=sb["cbias"][0:1, 2:3], scale=1.0),
                     reads=("pO", "cbias"), writes=("dnm",), dbg=f"ia{li}g{g}:d{w}_{h}")
                P.op("DVE", lambda e: e.reciprocal(out=c.rcp[:, 0:W], in_=c.dnm[:, 0:W]),
                     reads=("dnm",), writes=("rcp",), dbg=f"ia{li}g{g}:r{w}_{h}")
                P.op("DVE", lambda e: e.tensor_copy(out=c.rcpb[:, 0:W], in_=c.rcp[:, 0:W]),
                     reads=("rcp",), writes=("rcpb",), dbg=f"ia{li}g{g}:rb{w}_{h}")
                mmop(c, ps["pC"][0:64, 0:W], sb["cone_row"][0:1, 0:64], c.rcpb[:, 0:W],
                     True, True, ("cone_row", "rcpb", "pC"), ("pC",), f"ia{li}g{g}:bc{w}_{h}")
                P.op("ACT", lambda e: e.activation(out=sb["Ost"][:, 0:W], in_=ps["pO"][0:64, 0:W],
                                                   func=AF.Identity, bias=0.0, scale=1.0),
                     reads=("pO",), writes=("Ost",), dbg=f"ia{li}g{g}:oe{w}_{h}")
                P.op("DVE", lambda e, d=c.WoIn[b0:b0 + 64, hp * GT + qoff:hp * GT + qoff + W]:
                     e.tensor_tensor(out=d, in0=sb["Ost"][:, 0:W], in1=ps["pC"][0:64, 0:W], op=ALU.mult),
                     reads=("Ost", "pC"), writes=(f"WoIn{hp}",), dbg=f"ia{li}g{g}:n{w}_{h}")
        wto = stream_w_tiles(c, tn["bWo"][li], f"bWo{li}", DC, list(range(DC)),
                             "wr0", f"ia{li}g{g}wo")
        for oc in range(DC):
            pn, pres = (ps["pA"], "pA") if oc % 2 == 0 else (ps["pB"], "pB")
            for k in range(DC):
                ap, res = wto(oc, k)
                mmop(c, pn[:, 0:GT], ap,
                     c.WoIn[:, k * GT:(k + 1) * GT], k == 0, k == DC - 1,
                     (res, f"WoIn{k}", pres), (pres,), f"ia{li}g{g}:wo{oc}")
            P.op("DVE", lambda e, d=sb["yout"][:, oc * GT:(oc + 1) * GT],
                 a=yin_ap(c, buf, oc, False), p=pn[:, 0:GT]:
                 e.tensor_tensor(out=d, in0=a, in1=p, op=ALU.add),
                 reads=(f"yin{buf}_{oc}", pres), writes=(f"yo{oc}",), dbg=f"ia{li}g{g}:yu{oc}")
            P.op("ACT", lambda e, d=tn[ydst][oc * 128:(oc + 1) * 128, gsl],
                 s=sb["yout"][:, oc * GT:(oc + 1) * GT]: e.dma_start(out=d, in_=s),
                 reads=(f"yo{oc}",), writes=(f"{ydst}_g{g}",), dma="d_yo", dbg=f"ia{li}g{g}:ys{oc}")
        if PARANOID and oc == DC - 1:
            P.barrier()



def emit_inner_ffn(c, li, ysrc, ydst, final=False):
    P, tn, sb, ps = c.P, c.tn, c.sb, c.ps
    for nm, src in (("cb1", tn["b1c"][li]), ("cb2", tn["b2c"][li]),
                    ("cg", tn["gc"][li]), ("cbb", tn["bcb"][li])):
        P.op("SP", lambda e, d=sb[nm][:], s=src: e.dma_start(out=d, in_=s),
             reads=(), writes=(nm,), dma="d_sp", dbg=f"if{li}:{nm}")
    for g in range(NG):
        buf = g % 2
        gsl = slice(g * GT, (g + 1) * GT)
        emit_yin_load(c, buf, g, ysrc, f"if{li}g{g}")
        emit_sbf_cast(c, buf, f"if{li}g{g}")
        chunks = [(sb["sbf"][:, i * GT:(i + 1) * GT], f"sbf{i}") for i in range(DC)]
        ln_compute(c, chunks, f"if{li}g{g}")
        for ci in range(DC):
            P.op("DVE", lambda e, a=sb["sbf"][:, ci * GT:(ci + 1) * GT],
                 d=c.s2n[:, ci * GT:(ci + 1) * GT]:
                 e.tensor_tensor(out=d, in0=a, in1=sb["bcAt"][:], op=ALU.mult),
                 reads=(f"sbf{ci}", "bcAt"), writes=(f"s2n{ci}",), dbg=f"if{li}g{g}:nA{ci}")
            P.op("DVE", lambda e, d=c.s2n[:, ci * GT:(ci + 1) * GT]:
                 e.tensor_tensor(out=d, in0=d, in1=sb["bcBt"][:], op=ALU.add),
                 reads=("bcBt", f"s2n{ci}"), writes=(f"s2n{ci}",), dbg=f"if{li}g{g}:nB{ci}")
            P.op("DVE", lambda e, d=c.s2n[:, ci * GT:(ci + 1) * GT],
                 g1=sb["cg"][:, ci:ci + 1], b1_=sb["cbb"][:, ci:ci + 1]:
                 e.tensor_scalar(out=d, in0=d, scalar1=g1, scalar2=b1_,
                                 op0=ALU.mult, op1=ALU.add),
                 reads=("cg", "cbb", f"s2n{ci}"), writes=(f"s2n{ci}",), dbg=f"if{li}g{g}:nG{ci}")
        if getattr(c, "dev_io", False) and DBG_DUMP and li == c.n_layers - 1 and g in DBG_GROUPS:
            for ci in range(DC):
                P.op("SP", lambda e, d=tn["dbgS"][g][:, ci * GT:(ci + 1) * GT],
                     s=c.s2n[:, ci * GT:(ci + 1) * GT]: e.dma_start(out=d, in_=s),
                     reads=(f"s2n{ci}",), writes=(), dma="d_dbg", dbg="dbgS")
                P.op("SP", lambda e, d=tn["dbgF"][g][:, ci * GT:(ci + 1) * GT],
                     s=sb["sbf"][:, ci * GT:(ci + 1) * GT]: e.dma_start(out=d, in_=s),
                     reads=(f"sbf{ci}",), writes=(), dma="d_dbg", dbg="dbgF")
        wt = stream_w_tiles(c, tn["bW1P"][li], f"bW1P{li}", DC, list(range(2 * NJ)),
                            "wr0", f"if{li}g{g}w1")
        for j in range(NJ):
            for half, (pn, pres) in enumerate((( ps["pA"], "pA"), (ps["pB"], "pB"))):
                for k in range(DC):
                    ap, res = wt(2 * j + half, k)
                    mmop(c, pn[:, 0:GT], ap, c.s2n[:, k * GT:(k + 1) * GT],
                         k == 0, k == DC - 1, (res, f"s2n{k}", pres), (pres,),
                         f"if{li}g{g}:w1_{j}_{half}")
            P.op("DVE", lambda e, j_=j: e.tensor_scalar(
                out=sb["abuf"][:], in0=ps["pA"][:, 0:GT],
                scalar1=sb["cb1"][:, 2 * j_:2 * j_ + 1], scalar2=None, op0=ALU.add),
                reads=("pA", "cb1"), writes=("abuf",), dbg=f"if{li}g{g}:a{j}")
            P.op("ACT", lambda e, j_=j: e.activation(
                out=sb["gbuf"][:], in_=ps["pB"][:, 0:GT], func=AF.Gelu,
                bias=sb["cb1"][:, 2 * j_ + 1:2 * j_ + 2], scale=1.0),
                reads=("pB", "cb1"), writes=("gbuf",), dbg=f"if{li}g{g}:g{j}")
            P.op("DVE", lambda e, d=c.hbuf[:, j * GT:(j + 1) * GT]:
                 e.tensor_tensor(out=d, in0=sb["abuf"][:], in1=sb["gbuf"][:], op=ALU.mult),
                 reads=("abuf", "gbuf"), writes=(f"hbuf{j}",), dbg=f"if{li}g{g}:h{j}")
        if getattr(c, "dev_io", False) and DBG_DUMP and li == c.n_layers - 1 and g in DBG_GROUPS:
            for j in range(NJ):
                P.op("SP", lambda e, d=tn["dbgH"][g][:, j * GT:(j + 1) * GT],
                     s=c.hbuf[:, j * GT:(j + 1) * GT]: e.dma_start(out=d, in_=s),
                     reads=(f"hbuf{j}",), writes=(), dma="d_dbg", dbg="dbgH")
        for hh in range(2):
            for j in range(NJ):
                P.op("SP", lambda e, d=c.w2res[:, j * GT:(j + 1) * GT],
                     s=tn["bW2P"][li][j * 128:(j + 1) * 128, hh * GT:(hh + 1) * GT]:
                     e.dma_start(out=d, in_=s),
                     reads=(f"bW2P{li}",), writes=(f"w2r{j}",), dma="d_w", dbg=f"if{li}g{g}:w2l{j}")
            for oc2 in range(4):
                oc = hh * 4 + oc2
                pn, pres = ((ps["pC"], "pC"), (ps["pD"], "pD"), (ps["pS0"], "pS0"),
                            (ps["pS1"], "pS1"))[oc2]
                for j in range(NJ):
                    mmop(c, pn[:, 0:GT], c.w2res[:, j * GT + oc2 * 128:j * GT + (oc2 + 1) * 128],
                         c.hbuf[:, j * GT:(j + 1) * GT], j == 0, j == NJ - 1,
                         (f"w2r{j}", f"hbuf{j}", pres), (pres,), f"if{li}g{g}:w2_{oc}")
                P.op("DVE", lambda e, d=sb["yout"][:, oc * GT:(oc + 1) * GT],
                     a=yin_ap(c, buf, oc, False), p=pn[:, 0:GT], oc_=oc:
                     e.scalar_tensor_tensor(out=d, in0=p, scalar=sb["cb2"][:, oc_:oc_ + 1],
                                            in1=a, op0=ALU.add, op1=ALU.add),
                     reads=(pres, "cb2", f"yin{buf}_{oc}"), writes=(f"yo{oc}",),
                     dbg=f"if{li}g{g}:yu{oc}")
                if not final:
                    P.op("ACT", lambda e, d=tn[ydst][oc * 128:(oc + 1) * 128, gsl],
                         s=sb["yout"][:, oc * GT:(oc + 1) * GT]: e.dma_start(out=d, in_=s),
                         reads=(f"yo{oc}",), writes=(f"{ydst}_g{g}",), dma="d_yo",
                         dbg=f"if{li}g{g}:ys{oc}")
                    if PARANOID and oc == DC - 1:
                        P.barrier()
        if g % 4 == 3 and g != NG - 1 and not final:
            c.P.barrier()
        if final:
            for ci in range(DC):
                P.op("ACT", lambda e, d=c.xb[:, ci * GT:(ci + 1) * GT],
                     s=tn["X1"][ci * 128:(ci + 1) * 128, gsl]: e.dma_start(out=d, in_=s),
                     reads=("X1_g%d" % g,), writes=(f"xb{ci}",), dma="d_y", dbg=f"fin{g}:x{ci}")
                P.op("DVE", lambda e, a=sb["yout"][:, ci * GT:(ci + 1) * GT],
                     d=sb["sbf"][:, ci * GT:(ci + 1) * GT]: e.tensor_copy(out=d, in_=a),
                     reads=(f"yo{ci}",), writes=(f"sbf{ci}",), dbg=f"fin{g}:c{ci}")
            chunks = [(sb["sbf"][:, i * GT:(i + 1) * GT], f"sbf{i}") for i in range(DC)]
            ln_compute(c, chunks, f"fin{g}")
            for ci in range(DC):
                P.op("DVE", lambda e, d=sb["yout"][:, ci * GT:(ci + 1) * GT]:
                     e.tensor_tensor(out=d, in0=d, in1=sb["bcAt"][:], op=ALU.mult),
                     reads=(f"yo{ci}", "bcAt"), writes=(f"yo{ci}",), dbg=f"fin{g}:m{ci}")
                P.op("DVE", lambda e, d=sb["yout"][:, ci * GT:(ci + 1) * GT]:
                     e.tensor_tensor(out=d, in0=d, in1=sb["bcBt"][:], op=ALU.add),
                     reads=("bcBt", f"yo{ci}"), writes=(f"yo{ci}",), dbg=f"fin{g}:a{ci}")
                P.op("DVE", lambda e, d=sb["yout"][:, ci * GT:(ci + 1) * GT],
                     x=c.xb[:, ci * GT:(ci + 1) * GT]:
                     e.tensor_tensor(out=d, in0=d, in1=x, op=ALU.add),
                     reads=(f"xb{ci}", f"yo{ci}"), writes=(f"yo{ci}",), dbg=f"fin{g}:x{ci}")
                P.op("ACT", lambda e, d=tn["outT"][ci * 128:(ci + 1) * 128, gsl],
                     s=sb["yout"][:, ci * GT:(ci + 1) * GT]: e.dma_start(out=d, in_=s),
                     reads=(f"yo{ci}",), writes=(), dma="d_yo", dbg=f"fin{g}:s{ci}")
            if ci == DC - 1 and g % 4 == 3 and g != NG - 1:
                P.barrier()



def emit_phase_A(c, wcstate):
    P, tn, sb, ps = c.P, c.tn, c.sb, c.ps
    for i in range(2):
        ap = c.vst[i].rearrange("p (h z) -> p h z", z=65)[:, :, 64:65]
        P.op("DVE", lambda e, a=ap: e.memset(a, 1.0), reads=(), writes=(f"vst{i}",), dbg="vst1")
    for g in range(NG):
        buf = g % 2
        gsl = slice(g * GT, (g + 1) * GT)
        emit_yin_load(c, buf, g, "xT", f"oa{g}")
        emit_sbf_cast(c, buf, f"oa{g}", do_shift=False)
        for ti, tname in enumerate(("cos2o", "sin2o")):
            P.op("SP", lambda e, d=sb["ctabs"][:, ti * GT:(ti + 1) * GT],
                 s=tn[tname][:, gsl]: e.dma_start(out=d, in_=s),
                 reads=(), writes=(f"ctab{ti}",), dma="d_tab", dbg=f"oa{g}:t{ti}")
        chunks = [(sb["sbf"][:, i * GT:(i + 1) * GT], f"sbf{i}") for i in range(DC)]
        ln_compute(c, chunks, f"oa{g}")
        for ci in range(DC):
            P.op("DVE", lambda e, a=sb["sbf"][:, ci * GT:(ci + 1) * GT],
                 d=c.qsc[:, ci * GT:(ci + 1) * GT]:
                 e.tensor_tensor(out=d, in0=a, in1=sb["bcAt"][:], op=ALU.mult),
                 reads=(f"sbf{ci}", "bcAt"), writes=(f"qsc{ci}",), dbg=f"oa{g}:hA{ci}")
            P.op("DVE", lambda e, d=c.qsc[:, ci * GT:(ci + 1) * GT]:
                 e.tensor_tensor(out=d, in0=d, in1=sb["bcBt"][:], op=ALU.add),
                 reads=("bcBt", f"qsc{ci}"), writes=(f"qsc{ci}",), dbg=f"oa{g}:hB{ci}")
        wtk = stream_w_tiles(c, tn["bOW"][1], "bOW1", DC, list(range(DC)), "wr0", f"oa{g}k")
        for hp in range(HP):
            pn, pres = (ps["pA"], "pA") if hp % 2 == 0 else (ps["pB"], "pB")
            for k in range(DC):
                ap, res = wtk(hp, k)
                mmop(c, pn[:, 0:GT], ap, c.qsc[:, k * GT:(k + 1) * GT],
                     k == 0, k == DC - 1, (res, f"qsc{k}", pres), (pres,), f"oa{g}:kp{hp}")
            P.op("ACT", lambda e, i=pn[:, 0:GT]: e.activation(out=c.krp, in_=i,
                                                              func=AF.Identity, bias=0.0, scale=1.0),
                 reads=(pres,), writes=("krp",), dbg=f"oa{g}:ke{hp}")
            sl = hp % 2
            emit_rope(c, c.krp, "krp", c.kvst[sl], f"kvst{sl}",
                      sb["ctabs"][:, 0:GT], "ctab0", sb["ctabs"][:, GT:2 * GT], "ctab1",
                      f"oa{g}:kr{hp}")
            for beta in range(2):
                P.op("ACT", lambda e, d=tn["Ksc"][2 * hp + beta][:, gsl],
                     s=c.kvst[sl][beta * 64:(beta + 1) * 64, :]: e.dma_start(out=d, in_=s),
                     reads=(f"kvst{sl}",), writes=(f"Ksc_{2 * hp + beta}_g{g}",),
                     dma="d_kv", dbg=f"oa{g}:ks{hp}_{beta}")
        for k in range(DC):
            P.op("SP", lambda e, d=sb["wr1"][:, k * 1024:(k + 1) * 1024],
                 s=tn["bOW"][2][k * 128:(k + 1) * 128, :]: e.dma_start(out=d, in_=s),
                 reads=("bOW2",), writes=(f"wr1_{k}",), dma="d_w", dbg=f"oa{g}:wv{k}")
        for sub in range(4):
            sl = sub % 2
            for half in range(2):
                pn, pres = (ps["pS0"], "pS0") if half == 0 else (ps["pS1"], "pS1")
                for k in range(DC):
                    mmop(c, pn[:, 0:GT],
                         c.qsc[:, k * GT + sub * 128:k * GT + (sub + 1) * 128],
                         sb["wr1"][:, k * 1024 + half * GT:k * 1024 + (half + 1) * GT],
                         k == 0, k == DC - 1,
                         (f"qsc{k}", f"wr1_{k}", pres), (pres,), f"oa{g}:v{sub}_{half}")
                src = pn[:, 0:GT].rearrange("p (h z) -> p h z", z=64)
                dst = c.vst[sl].rearrange("p (h z) -> p h z", z=65)[:, half * 8:(half + 1) * 8, 0:64]
                P.op("DVE", lambda e, s=src, d=dst: e.tensor_copy(out=d, in_=s),
                     reads=(pres,), writes=(f"vst{sl}",), dbg=f"oa{g}:ve{sub}_{half}")
            P.op("ACT", lambda e, d=tn["Vsc"][4 * g + sub], s=c.vst[sl]:
                 e.dma_start(out=d, in_=s),
                 reads=(f"vst{sl}",), writes=(f"Vsc_{4 * g + sub}",), dma="d_kv", dbg=f"oa{g}:vs{sub}")
        drain_wc(c, wcstate, budget=10)



def emit_phase_B(c, wcstate):
    P, tn, sb, ps = c.P, c.tn, c.sb, c.ps
    for j in range(4):
        P.op("SP", lambda e, d=c.cmask_o[j], s=tn["mask_o"][j]: e.dma_start(out=d, in_=s),
             reads=(), writes=("cmask_o",), dma="d_sp", dbg=f"ob:mo{j}")
    for qt in range(NG):
        buf = qt % 2
        gsl = slice(qt * GT, (qt + 1) * GT)
        emit_yin_load(c, buf, qt, "xT", f"ob{qt}")
        emit_sbf_cast(c, buf, f"ob{qt}", do_shift=False)
        for ti, tname in enumerate(("cos2o", "sin2o")):
            P.op("SP", lambda e, d=sb["ctabs"][:, ti * GT:(ti + 1) * GT],
                 s=tn[tname][:, gsl]: e.dma_start(out=d, in_=s),
                 reads=(), writes=(f"ctab{ti}",), dma="d_tab", dbg=f"ob{qt}:t{ti}")
        chunks = [(sb["sbf"][:, i * GT:(i + 1) * GT], f"sbf{i}") for i in range(DC)]
        ln_compute(c, chunks, f"ob{qt}")
        for ci in range(DC):
            P.op("DVE", lambda e, a=sb["sbf"][:, ci * GT:(ci + 1) * GT],
                 d=c.qsc[:, ci * GT:(ci + 1) * GT]:
                 e.tensor_tensor(out=d, in0=a, in1=sb["bcAt"][:], op=ALU.mult),
                 reads=(f"sbf{ci}", "bcAt"), writes=(f"qsc{ci}", f"qpad{ci}"), dbg=f"ob{qt}:hA{ci}")
            P.op("DVE", lambda e, d=c.qsc[:, ci * GT:(ci + 1) * GT]:
                 e.tensor_tensor(out=d, in0=d, in1=sb["bcBt"][:], op=ALU.add),
                 reads=("bcBt", f"qsc{ci}"), writes=(f"qsc{ci}", f"qpad{ci}"), dbg=f"ob{qt}:hB{ci}")
        wtq = stream_w_tiles(c, tn["bOW"][0], "bOW0", DC, list(range(DC)), "wr0", f"ob{qt}q")
        for hp in range(HP):
            pn, pres = (ps["pA"], "pA") if hp % 2 == 0 else (ps["pB"], "pB")
            for k in range(DC):
                ap, res = wtq(hp, k)
                mmop(c, pn[:, 0:GT], ap, c.qsc[:, k * GT:(k + 1) * GT],
                     k == 0, k == DC - 1, (res, f"qsc{k}", pres), (pres,), f"ob{qt}:qp{hp}")
            P.op("ACT", lambda e, i=pn[:, 0:GT],
                 o=sb["arBF"][:, 4096 + hp * GT:4096 + (hp + 1) * GT]:
                 e.activation(out=o, in_=i, func=AF.Identity, bias=0.0, scale=1.0),
                 reads=(pres,), writes=(f"qsc_hi{hp}",), dbg=f"ob{qt}:qe{hp}")
        for hp in range(HP):
            emit_rope_padq(c, sb["arBF"][:, 4096 + hp * GT:4096 + (hp + 1) * GT],
                           f"qsc_hi{hp}", hp,
                           sb["ctabs"][:, 0:GT], "ctab0", sb["ctabs"][:, GT:2 * GT], "ctab1",
                           f"ob{qt}:qr{hp}")
        nkt = 4 * qt + 4
        AVP = (("pO", ps["pO"]), ("pC", ps["pC"]), ("pD", ps["pD"]), ("pR", ps["pR"]))
        for hg in range(4):
            heads = [4 * hg + i for i in range(4)]
            for kt in range(nkt):
                ksl = kt % 2
                for i, h in enumerate(heads):
                    b0 = (h % 2) * 64
                    P.op("ACT", lambda e, d=c.kvst[ksl][b0:b0 + 64, (i // 2) * 128:(i // 2) * 128 + 128],
                         s=tn["Ksc"][h][:, kt * 128:(kt + 1) * 128]: e.dma_start(out=d, in_=s),
                         reads=(f"Ksc_{h}_g{kt // 4}",), writes=(f"kvst{ksl}",),
                         dma="d_kv", dbg=f"ob{qt}:kl{hg}_{kt}_{i}")
                P.op("ACT", lambda e, d=c.vst[ksl], s=tn["Vsc"][kt]:
                     e.dma_start(out=d, in_=s),
                     reads=(f"Vsc_{kt}",), writes=(f"vst{ksl}",), dma="d_kv", dbg=f"ob{qt}:vl{hg}_{kt}")
                for i, h in enumerate(heads):
                    pn, pres = (ps["pS0"], "pS0") if i % 2 == 0 else (ps["pS1"], "pS1")
                    mmop(c, pn[:, 0:GT],
                         c.kvst[ksl][:, (i // 2) * 128:(i // 2) * 128 + 128],
                         qpad_ap(c, h),
                         True, True, (f"kvst{ksl}", f"qpad{h}", pres), (pres,),
                         f"ob{qt}:S{hg}_{kt}_{i}")
                    P.op("ACT", lambda e, o=c.Pt[:, (i % 2) * GT:(i % 2 + 1) * GT], p=pn[:, 0:GT]:
                         e.activation(out=o, in_=p, func=AF.Exp, bias=0.0, scale=0.125),
                         reads=(pres,), writes=(f"Pt{i % 2}",), dbg=f"ob{qt}:x{hg}_{kt}_{i}")
                    if kt >= 4 * qt:
                        P.op("DVE", lambda e, a=c.Pt[:, (i % 2) * GT:(i % 2 + 1) * GT],
                             m=c.cmask_o[kt - 4 * qt]:
                             e.tensor_tensor(out=a, in0=a, in1=m, op=ALU.min),
                             reads=(f"Pt{i % 2}", "cmask_o"), writes=(f"Pt{i % 2}",),
                             dbg=f"ob{qt}:m{hg}_{kt}_{i}")
                    prv, pnv = AVP[i]
                    mmop(c, pnv[0:65, 0:GT], c.vst[ksl][:, h * 65:(h + 1) * 65],
                         c.Pt[:, (i % 2) * GT:(i % 2 + 1) * GT],
                         kt == 0, kt == nkt - 1,
                         (f"vst{ksl}", f"Pt{i % 2}", prv), (prv,), f"ob{qt}:A{hg}_{kt}_{i}")
            for i, h in enumerate(heads):
                hp, beta = h // 2, h % 2
                b0 = beta * 64
                prv, pnv = AVP[i]
                P.op("ACT", lambda e, p=pnv[64:65, 0:GT]: e.activation(
                    out=c.dnm, in_=p, func=AF.Identity, bias=sb["cbias"][0:1, 2:3], scale=1.0),
                    reads=(prv, "cbias"), writes=("dnm",), dbg=f"ob{qt}:d{hg}_{i}")
                P.op("DVE", lambda e: e.reciprocal(out=c.rcp, in_=c.dnm),
                     reads=("dnm",), writes=("rcp",), dbg=f"ob{qt}:r{hg}_{i}")
                P.op("DVE", lambda e: e.tensor_copy(out=c.rcpb, in_=c.rcp),
                     reads=("rcp",), writes=("rcpb",), dbg=f"ob{qt}:rb{hg}_{i}")
                mmop(c, ps["pS0"][0:64, 0:GT], sb["cone_row"][0:1, 0:64], c.rcpb,
                     True, True, ("cone_row", "rcpb", "pS0"), ("pS0",), f"ob{qt}:bc{hg}_{i}")
                P.op("ACT", lambda e, p=pnv[0:64, 0:GT]: e.activation(
                    out=sb["Ost"][:], in_=p, func=AF.Identity, bias=0.0, scale=1.0),
                    reads=(prv,), writes=("Ost",), dbg=f"ob{qt}:oe{hg}_{i}")
                P.op("DVE", lambda e, d=c.WoIn[b0:b0 + 64, hp * GT:(hp + 1) * GT]:
                     e.tensor_tensor(out=d, in0=sb["Ost"][:], in1=ps["pS0"][0:64, 0:GT], op=ALU.mult),
                     reads=("Ost", "pS0"), writes=(f"WoIn{hp}",), dbg=f"ob{qt}:n{hg}_{i}")
        wto = stream_w_tiles(c, tn["bOW"][3], "bOW3", DC, list(range(DC)),
                             "wr0", f"ob{qt}wo")
        for oc in range(DC):
            pn, pres = (ps["pA"], "pA") if oc % 2 == 0 else (ps["pB"], "pB")
            for k in range(DC):
                ap, res = wto(oc, k)
                mmop(c, pn[:, 0:GT], ap,
                     c.WoIn[:, k * GT:(k + 1) * GT], k == 0, k == DC - 1,
                     (res, f"WoIn{k}", pres), (pres,), f"ob{qt}:wo{oc}")
            P.op("ACT", lambda e, d=c.zb[:, oc * GT:(oc + 1) * GT], p=pn[:, 0:GT]:
                 e.activation(out=d, in_=p, func=AF.Identity, bias=0.0, scale=1.0),
                 reads=(pres,), writes=(f"zb{oc}",), dbg=f"ob{qt}:ze{oc}")
            P.op("DVE", lambda e, d=sb["sbf"][:, oc * GT:(oc + 1) * GT],
                 s=c.zb[:, oc * GT:(oc + 1) * GT]: e.tensor_copy(out=d, in_=s),
                 reads=(f"zb{oc}",), writes=(f"sbf{oc}",), dbg=f"ob{qt}:zc{oc}")
        chunks = [(sb["sbf"][:, i * GT:(i + 1) * GT], f"sbf{i}") for i in range(DC)]
        ln_compute(c, chunks, f"ob{qt}z")
        for ci in range(DC):
            P.op("DVE", lambda e, d=c.zb[:, ci * GT:(ci + 1) * GT]:
                 e.tensor_tensor(out=d, in0=d, in1=sb["bcAt"][:], op=ALU.mult),
                 reads=("bcAt", f"zb{ci}"), writes=(f"zb{ci}",), dbg=f"ob{qt}:fA{ci}")
            P.op("DVE", lambda e, d=c.zb[:, ci * GT:(ci + 1) * GT]:
                 e.tensor_tensor(out=d, in0=d, in1=sb["bcBt"][:], op=ALU.add),
                 reads=("bcBt", f"zb{ci}"), writes=(f"zb{ci}",), dbg=f"ob{qt}:fB{ci}")
            P.op("DVE", lambda e, d=c.zb[:, ci * GT:(ci + 1) * GT],
                 a=yin_ap(c, buf, ci, False): e.tensor_tensor(out=d, in0=d, in1=a, op=ALU.add),
                 reads=(f"yin{buf}_{ci}", f"zb{ci}"), writes=(f"zb{ci}",), dbg=f"ob{qt}:fx{ci}")
            P.op("ACT", lambda e, d=tn["X1"][ci * 128:(ci + 1) * 128, gsl],
                 s=c.zb[:, ci * GT:(ci + 1) * GT]: e.dma_start(out=d, in_=s),
                 reads=(f"zb{ci}",), writes=(f"X1_g{qt}",), dma="d_yo", dbg=f"ob{qt}:xs{ci}")
        drain_wc(c, wcstate, budget=14)
        if qt % 2 == 1 and qt != NG - 1:
            P.barrier()



def emit_all(c, n_layers, with_outer):
    P, tn, sb = c.P, c.tn, c.sb

    for i, (nm, t) in enumerate(sorted(sb.items())):
        eng = "DVE" if i % 2 == 0 else "PL"
        P.op(eng, lambda e, a=t[:]: e.memset(a, 0.0), reads=(), writes=(nm + "_init",),
             dbg=f"init:{nm}")
    P.barrier()

    def ld(dst_ap, src_ap, res):
        P.op("SP", lambda e, d=dst_ap, s=src_ap: e.dma_start(out=d, in_=s),
             reads=(), writes=(res,), dma="d_sp", dbg="ld:" + res)

    ld(sb["cones_col"][:], tn["ones_col"], "cones_col")
    ld(sb["cone_row"][:], tn["one_row"], "cone_row")
    ld(sb["cneg_row"][:], tn["neg_row"], "cneg_row")
    ld(sb["cRT"][:], tn["RT"], "cRT")
    ld(sb["cmask_i"][:], tn["mask_i"], "cmask_i")
    ld(sb["cbias"][:], tn["biasc"], "cbias")
    ld(sb["cnormI"][:], tn["normI"], "cnormI")
    wcstate = {"jobs": weight_cast_jobs(c, n_layers), "pos": (0, 0), "slot": 0}
    drain_wc(c, wcstate, budget=36)

    if with_outer:
        emit_phase_A(c, wcstate)
        P.barrier()
        emit_phase_B(c, wcstate)
        P.barrier()
        emit_pass_C(c)
        P.barrier()
    while wcstate["jobs"]:
        drain_wc(c, wcstate, budget=50)
    P.barrier()
    ysrc, ydst = "Y0", "Y1"
    for li in range(n_layers):
        emit_inner_attn(c, li, ysrc, ydst)
        P.barrier()
        final = (li == n_layers - 1)
        emit_inner_ffn(c, li, ydst, ysrc, final=final)
        P.barrier()



_CACHED = {}


def kernel(**inputs):
    prep = host_prepare(inputs)
    nc, stats = build_program()
    x = np.asarray(inputs["L0_attn_x"], dtype=np.float32)
    shared = {k: v for k, v in prep.items()}
    in_maps = []
    for i in range(8):
        m = dict(shared)
        m["xT"] = np.ascontiguousarray(x[i].T)
        in_maps.append(m)
    res = run_bass_kernel_spmd(nc, in_maps, core_ids=list(range(8)))
    out = np.stack([np.asarray(r["outT"]).T for r in res.results], axis=0)
    return out
